# version 06
# baseline (speedup 1.0000x reference)
import numpy as np

import concourse.bass as bass
import concourse.mybir as mybir
import concourse.library_config as libcfg
from concourse.bass_utils import run_bass_kernel_spmd
from bass_rust import ReduceOp

F32 = mybir.dt.float32
BF16 = mybir.dt.bfloat16
ALU = mybir.AluOpType
AF = mybir.ActivationFunctionType
AX = mybir.AxisListType

NCORE = 8
DIM = 2048
TOKC = 1024
HALF = 512
NH = 16
D = 64
TWOH = 32
DI = 8192
LAM_INIT = 0.8 - 0.6 * float(np.exp(-0.3 * 12))
SCALE = D ** (-0.5)
EPS = 1e-5
QB = 128.0
C_FW = 1.0 / float(np.sqrt(DI))


def _perm_qk():
    p = np.zeros(DIM, dtype=np.int64)
    for i in range(TWOH):
        pp, h = i // 16, i % 16
        for d in range(D):
            p[i * D + d] = h * 128 + 2 * d + pp
    return p


def _perm_v():
    p = np.zeros(DIM, dtype=np.int64)
    for e in range(128):
        for h in range(NH):
            p[e * NH + h] = h * 128 + e
    return p


def _perm_wo_rows():
    p = np.zeros(DIM, dtype=np.int64)
    for a in range(NH):
        for e in range(128):
            p[a + 16 * (e & 1) + 32 * (e >> 1)] = a * 128 + e
    return p


def _hadamard(n):
    h = np.array([[1.0]], dtype=np.float32)
    while h.shape[0] < n:
        h = np.block([[h, h], [h, -h]])
    return np.ascontiguousarray(h.astype(np.float32))


class Sem:
    def __init__(self, nc, name):
        self.h = nc.semaphore(name).__enter__()
        self.v = 0


def build_program():
    nc = bass.Bass("TRN2", target_bir_lowering=False)

    def din(name, shape, dt=F32):
        return nc.dram_tensor(name, list(shape), dt, kind="ExternalInput").ap()

    xT = din("xT", [DIM, TOKC])
    wqT = din("wqT", [DIM, DIM])
    wkT = din("wkT", [DIM, DIM])
    wvT = din("wvT", [DIM, DIM])
    woT = din("woT", [DIM, DIM])
    wgT = din("wgT", [DIM, DI])
    wuT = din("wuT", [DIM, DI])
    wdT = din("wdT", [DI, DIM])
    wmod = din("wmod", [DIM, 6 * DIM])
    cvec = din("cvec", [1, DIM])
    bo = din("bo", [1, DIM])
    lq1 = din("lq1", [1, D])
    lq2 = din("lq2", [1, D])
    lk1 = din("lk1", [1, D])
    lk2 = din("lk2", [1, D])
    eye = din("eye", [128, 128])
    had = din("had", [128, 128])
    onesc = din("onesc", [1, 128])
    onesp = din("onesp", [128, 1])

    y = nc.dram_tensor("y", [DIM, TOKC], F32, kind="ExternalOutput").ap()
    modscr = nc.dram_tensor("modscr", [1, 6 * DIM], F32, kind="Internal").ap()
    x2scr = nc.dram_tensor("x2scr", [DIM, TOKC], F32, kind="Internal").ap()
    vscr = nc.dram_tensor("vscr", [128, 8192], F32, kind="Internal").ap()
    tG = nc.dram_tensor("tG", [128, 64 * 2048], BF16, kind="Internal").ap()
    tU = nc.dram_tensor("tU", [128, 64 * 2048], BF16, kind="Internal").ap()
    tD = nc.dram_tensor("tD", [128, 64 * 2048], BF16, kind="Internal").ap()

    def sball(name, shape, dt=F32):
        return nc.sbuf_tensor(name, list(shape), dt).__enter__()

    AR1 = sball("AR1", [128, 16384])
    AR2 = sball("AR2", [128, 16384])
    AR3 = sball("AR3", [128, 8192])
    AR4 = sball("AR4", [128, 8192])
    ident = sball("ident", [128, 128])
    identb = sball("identb", [128, 128], BF16)
    hadt = sball("hadt", [128, 128])
    onescr = sball("onescr", [1, 128])
    onespc = sball("onespc", [128, 1])
    cfm = sball("cfm", [128, 16])
    bofm = sball("bofm", [128, 16])
    mvec = sball("mvec", [128, 96])
    lrow = sball("lrow", [1, 256])
    lam1 = sball("lam1", [1, 16])
    ppt = sball("ppt", [128, 16])
    redp = sball("redp", [128, 96])
    zq0t = sball("zq0t", [128, 256])
    qt1 = sball("qt1", [128, 512])
    qt2 = sball("qt2", [128, 512])
    qt3 = sball("qt3", [128, 512], mybir.dt.int32)

    pbk = [nc.psum_tensor(f"pbk{i}", [128, 512], F32).__enter__() for i in range(8)]

    rowb = AR1[0:1, 0:6 * DIM]
    rowc = AR2[0:1, 0:6 * DIM]
    srow = AR4[0:1, 6144:7168]
    qrow = AR4[0:1, 7168:8192]
    arow = AR1[0:1, 12288:13312]
    brow = AR1[0:1, 13312:14336]
    a2row = AR1[0:1, 14336:15360]
    b2row = AR1[0:1, 15360:16384]

    sPE = Sem(nc, "sPE")
    sDV = Sem(nc, "sDV")
    sAC = Sem(nc, "sAC")
    sPL = Sem(nc, "sPL")

    class DPool:
        def __init__(self, name, n):
            self.sems = [Sem(nc, f"{name}{i}") for i in range(n)]
            self.i = 0

        def load(self, q, out, in_, waits=(), slow=False):
            s = self.sems[self.i % len(self.sems)]
            self.i += 1
            q.wait_ge(s.h, s.v)
            for h, v in waits:
                q.wait_ge(h, v)
            kw = dict(allow_slow_non_contiguous=True) if slow else {}
            q.dma_start(out=out, in_=in_, **kw).then_inc(s.h, 16)
            s.v += 16
            return (s.h, s.v)

        def toks(self):
            return [(s.h, s.v) for s in self.sems]

    PL = DPool("pl", 8)
    PO = DPool("po", 4)

    def f32v(ar, c0, n):
        return ar[:, c0:c0 + n]

    def bfv(ar, c0f32, nbf):
        return ar.ap().bitcast(BF16)[:, 2 * c0f32:2 * c0f32 + nbf]

    def fence():
        vals = ([(s.h, s.v) for s in (sPE, sDV, sAC, sPL) if s.v > 0]
                + [(h, v) for (h, v) in PL.toks() + PO.toks() if v > 0])
        with nc.Block() as blk:
            @blk.tensor
            def _(e):
                for h, v in vals:
                    e.wait_ge(h, v)

            @blk.vector
            def _(e):
                for h, v in vals:
                    e.wait_ge(h, v)

            @blk.scalar
            def _(e):
                for h, v in vals:
                    e.wait_ge(h, v)

            @blk.gpsimd
            def _(e):
                for h, v in vals:
                    e.wait_ge(h, v)

            @blk.sync
            def _(e):
                for h, v in vals:
                    e.wait_ge(h, v)

    with nc.Block() as blk:
        @blk.sync
        def _(q):
            for dst, src in [(ident[:], eye[:]), (hadt[:], had[:]),
                             (onescr[:], onesc[:]), (onespc[:], onesp[:]),
                             (lrow[:, 0:64], lq1[:]), (lrow[:, 64:128], lq2[:]),
                             (lrow[:, 128:192], lk1[:]), (lrow[:, 192:256], lk2[:])]:
                PL.load(q, dst, src)
            PL.load(q, cfm[:], cvec.rearrange("o (k p) -> (o p) k", p=128), slow=True)
            PL.load(q, bofm[:], bo.rearrange("o (k p) -> (o p) k", p=128), slow=True)

    fence()
    with nc.Block() as blk:
        @blk.vector
        def _(v):
            v.tensor_copy(identb[:], ident[:])
            v.tensor_tensor(lrow[:, 0:64], lrow[:, 0:64], lrow[:, 128:192], op=ALU.mult)
            v.drain()
            v.tensor_tensor(lrow[:, 64:128], lrow[:, 64:128], lrow[:, 192:256], op=ALU.mult)
            v.drain()
            v.tensor_reduce(lam1[:, 0:1], lrow[:, 0:64], axis=AX.X, op=ALU.add)
            v.drain()
            v.tensor_reduce(lam1[:, 1:2], lrow[:, 64:128], axis=AX.X, op=ALU.add)
            v.drain()
            v.drain().then_inc(sDV.h, 1); sDV.v += 1

        @blk.scalar
        def _(s):
            s.wait_ge(sDV.h, sDV.v)
            s.activation(lam1[:, 2:4], lam1[:, 0:2], AF.Exp)
            s.drain()
            s.drain().then_inc(sAC.h, 1); sAC.v += 1

        @blk.vector
        def _(v):
            v.wait_ge(sAC.h, sAC.v)
            v.tensor_tensor(lam1[:, 4:5], lam1[:, 3:4], lam1[:, 2:3], op=ALU.subtract)
            v.drain()
            v.tensor_scalar(lam1[:, 4:5], lam1[:, 4:5], -LAM_INIT, None, op0=ALU.add)
            v.drain()
            v.drain().then_inc(sDV.h, 1); sDV.v += 1

        @blk.tensor
        def _(t):
            t.wait_ge(sDV.h, sDV.v)
            t.matmul(pbk[3][:, 0:1], onescr[:], lam1[0:1, 4:5], start=True, stop=True)
            t.drain().then_inc(sPE.h, 1); sPE.v += 1

        @blk.vector
        def _(v):
            v.wait_ge(sPE.h, sPE.v)
            v.tensor_copy(ppt[:, 0:1], pbk[3][:, 0:1])
            v.drain().then_inc(sDV.h, 1); sDV.v += 1

    with nc.Block() as blk:
        @blk.scalar
        def _(s):
            s.activation(cfm[:], cfm[:], AF.Silu)
            s.drain()
            s.drain().then_inc(sAC.h, 1); sAC.v += 1

    pe0 = sPE.v
    dv0 = sDV.v
    wm_toks = []
    with nc.Block() as blk:
        @blk.sync
        def _(q):
            for m in range(24):
                waits = [(sPE.h, pe0 + m - 1)] if m >= 2 else []
                base = (m % 2) * 8192
                wm_toks.append(PL.load(
                    q, f32v(AR2, base, 8192).rearrange("p (k c) -> p k c", c=512),
                    wmod[:, m * 512:(m + 1) * 512].rearrange("(k p) c -> p k c", p=128),
                    waits=waits))

        @blk.tensor
        def _(t):
            t.wait_ge(sAC.h, sAC.v)
            for m in range(24):
                t.wait_ge(*wm_toks[m])
                if m >= 2:
                    t.wait_ge(sDV.h, dv0 + m - 1)
                base = (m % 2) * 8192
                bank = pbk[m % 2]
                for k in range(16):
                    t.matmul(bank[0:1, :], cfm[:, k:k + 1], f32v(AR2, base + k * 512, 512),
                             start=(k == 0), stop=(k == 15))
                t.drain().then_inc(sPE.h, 1); sPE.v += 1

        @blk.vector
        def _(v):
            for m in range(24):
                v.wait_ge(sPE.h, pe0 + m + 1)
                v.tensor_copy(rowb[:, m * 512:(m + 1) * 512], pbk[m % 2][0:1, :])
                v.drain()
                v.drain().then_inc(sDV.h, 1); sDV.v += 1

    fence()
    with nc.Block() as blk:
        @blk.vector
        def _(v):
            v.tensor_copy(rowc[:, 0:DIM], rowb[:, 0:DIM])
            v.tensor_scalar(rowc[:, DIM:2 * DIM], rowb[:, DIM:2 * DIM], 1.0, None, op0=ALU.add)
            v.tensor_copy(rowc[:, 3 * DIM:4 * DIM], rowb[:, 3 * DIM:4 * DIM])
            v.tensor_scalar(rowc[:, 4 * DIM:5 * DIM], rowb[:, 4 * DIM:5 * DIM], 1.0, None, op0=ALU.add)
            for o in (2, 5):
                g0 = rowb[:, o * DIM:(o + 1) * DIM]
                gc = rowc[:, o * DIM:(o + 1) * DIM]
                gneg = rowb[:, 0:DIM]
                v.tensor_scalar(gc, g0, 0.0, 2.0, op0=ALU.is_ge, op1=ALU.mult)
                v.drain()
                v.tensor_scalar(gc, gc, -1.0, None, op0=ALU.add)
                v.drain()
                v.tensor_scalar(gneg, g0, -1.0, None, op0=ALU.mult)
                v.drain()
                v.tensor_tensor(g0, g0, gneg, op=ALU.max)
                v.drain()
                v.tensor_scalar(g0, g0, 0.25, None, op0=ALU.max)
                v.drain()
                v.tensor_tensor(gc, gc, g0, op=ALU.mult)
                v.drain()
            v.drain().then_inc(sDV.h, 1); sDV.v += 1

        @blk.sync
        def _(q):
            spill = PL.load(q, modscr[:], rowc, waits=[(sDV.h, sDV.v)])
            PL.load(q, mvec[:],
                    modscr.rearrange("o (s k p) -> (o p) (s k)", p=128, k=16),
                    waits=[spill], slow=True)

    dv0 = sDV.v
    sc_toks = []
    with nc.Block() as blk:
        @blk.sync
        def _(q):
            for i in range(32):
                waits = [(sDV.h, dv0 + i - 1)] if i >= 2 else []
                src = wgT[i * 128:(i + 1) * 128, :] if i < 16 else wuT[(i - 16) * 128:(i - 15) * 128, :]
                sc_toks.append(PL.load(q, f32v(AR1, (i % 2) * 8192, 8192), src, waits=waits))
            for j in range(16):
                waits = [(sDV.h, dv0 + 31 + j)]
                sc_toks.append(PL.load(
                    q, f32v(AR1, (j % 2) * 8192, 8192).rearrange("p (a c) -> p a c", c=2048),
                    wdT[j * 512:(j + 1) * 512, :].rearrange("(a p) c -> p a c", p=128),
                    waits=waits))

        @blk.vector
        def _(v):
            for i in range(48):
                v.wait_ge(*sc_toks[i])
                v.tensor_reduce(redp[:, i:i + 1], f32v(AR1, (i % 2) * 8192, 8192),
                                axis=AX.X, op=ALU.add, apply_absolute_value=True)
                v.drain()
                v.drain().then_inc(sDV.h, 1); sDV.v += 1

    fence()
    with nc.Block() as blk:
        @blk.vector
        def _(v):
            v.tensor_reduce(redp[:, 48:49], redp[:, 0:16], axis=AX.X, op=ALU.add)
            v.tensor_reduce(redp[:, 49:50], redp[:, 16:32], axis=AX.X, op=ALU.add)
            v.tensor_reduce(redp[:, 50:51], redp[:, 32:48], axis=AX.X, op=ALU.add)
            v.drain()
            v.drain().then_inc(sDV.h, 1); sDV.v += 1

        @blk.tensor
        def _(t):
            t.wait_ge(sDV.h, sDV.v)
            for j in range(3):
                t.matmul(pbk[4][0:1, j:j + 1], onespc[:], redp[:, 48 + j:49 + j],
                         start=True, stop=True)
            t.drain().then_inc(sPE.h, 1); sPE.v += 1

        @blk.vector
        def _(v):
            v.wait_ge(sPE.h, sPE.v)
            v.tensor_scalar(lam1[:, 5:8], pbk[4][0:1, 0:3], 1.0 / (DIM * DI), EPS,
                            op0=ALU.mult, op1=ALU.max)
            v.drain()
            v.tensor_scalar(lam1[:, 8:11], lam1[:, 5:8], 0.5, None, op0=ALU.mult)
            v.drain()
            v.tensor_scalar(lam1[:, 11:14], lam1[:, 8:11], -1.0, None, op0=ALU.mult)
            v.drain()
            v.drain().then_inc(sDV.h, 1); sDV.v += 1

        @blk.tensor
        def _(t):
            t.wait_ge(sDV.h, sDV.v)
            t.matmul(pbk[3][:, 0:9], onescr[:], lam1[0:1, 5:14], start=True, stop=True)
            t.drain().then_inc(sPE.h, 1); sPE.v += 1

        @blk.vector
        def _(v):
            v.wait_ge(sPE.h, sPE.v)
            v.tensor_copy(ppt[:, 1:10], pbk[3][:, 0:9])
            v.drain().then_inc(sDV.h, 1); sDV.v += 1

    fence()
    pe0 = sPE.v
    ac0 = sAC.v
    x_toks = []
    with nc.Block() as blk:
        @blk.sync
        def _(q):
            for ci in range(16):
                waits = [(sPE.h, pe0 + ci - 1)] if ci >= 2 else []
                x_toks.append(PL.load(q, f32v(AR4, (ci % 2) * 1024, 1024),
                                      xT[ci * 128:(ci + 1) * 128, :], waits=waits))

        @blk.scalar
        def _(s):
            for ci in range(16):
                s.wait_ge(*x_toks[ci])
                s.activation(f32v(AR4, 2048 + (ci % 2) * 1024, 1024),
                             f32v(AR4, (ci % 2) * 1024, 1024), AF.Square)
                s.drain()
                s.drain().then_inc(sAC.h, 1); sAC.v += 1

        @blk.tensor
        def _(t):
            for ci in range(16):
                t.wait_ge(sAC.h, ac0 + ci + 1)
                xc = (ci % 2) * 1024
                for hh in range(2):
                    t.matmul(pbk[4 + hh][0:1, :], onespc[:], f32v(AR4, xc + hh * 512, 512),
                             start=(ci == 0), stop=(ci == 15))
                    t.matmul(pbk[6 + hh][0:1, :], onespc[:], f32v(AR4, 2048 + xc + hh * 512, 512),
                             start=(ci == 0), stop=(ci == 15))
                t.drain().then_inc(sPE.h, 1); sPE.v += 1

    fence()

    def emit_ln_rows(adst_a, adst_b, dst_bc, from_psum):
        with nc.Block() as blk:
            @blk.vector
            def _(v):
                if from_psum:
                    v.tensor_scalar(srow[:, 0:512], pbk[4][0:1, :], 1.0 / DIM, None, op0=ALU.mult)
                    v.tensor_scalar(srow[:, 512:1024], pbk[5][0:1, :], 1.0 / DIM, None, op0=ALU.mult)
                    v.tensor_scalar(qrow[:, 0:512], pbk[6][0:1, :], 1.0 / DIM, None, op0=ALU.mult)
                    v.tensor_scalar(qrow[:, 512:1024], pbk[7][0:1, :], 1.0 / DIM, None, op0=ALU.mult)
                    v.drain()
                v.tensor_tensor(rowb[0:1, 0:1024], srow[:, :], srow[:, :], op=ALU.mult)
                v.drain()
                v.tensor_tensor(qrow[:, :], qrow[:, :], rowb[0:1, 0:1024], op=ALU.subtract)
                v.drain()
                v.tensor_scalar(qrow[:, :], qrow[:, :], EPS, None, op0=ALU.add)
                v.drain()
                v.reciprocal(qrow[:, :], qrow[:, :])
                v.drain()
                v.drain().then_inc(sDV.h, 1); sDV.v += 1

            @blk.scalar
            def _(s):
                s.wait_ge(sDV.h, sDV.v)
                s.activation(adst_a[:, :], qrow[:, :], AF.Sqrt)
                s.drain()
                s.drain().then_inc(sAC.h, 1); sAC.v += 1

            @blk.vector
            def _(v):
                v.wait_ge(sAC.h, sAC.v)
                v.tensor_tensor(adst_b[:, :], srow[:, :], adst_a[:, :], op=ALU.mult)
                v.drain()
                v.tensor_scalar(adst_b[:, :], adst_b[:, :], -1.0, None, op0=ALU.mult)
                v.drain()
                v.drain().then_inc(sDV.h, 1); sDV.v += 1

            @blk.tensor
            def _(t):
                t.wait_ge(sDV.h, sDV.v)
                for hh in range(2):
                    t.matmul(pbk[hh][:, :], onescr[:], adst_a[:, hh * 512:(hh + 1) * 512],
                             start=True, stop=True)
                    t.matmul(pbk[2 + hh][:, :], onescr[:], adst_b[:, hh * 512:(hh + 1) * 512],
                             start=True, stop=True)
                t.drain().then_inc(sPE.h, 1); sPE.v += 1

            @blk.vector
            def _(v):
                v.wait_ge(sPE.h, sPE.v)
                for hh in range(2):
                    v.tensor_copy(f32v(AR4, dst_bc + hh * 512, 512), pbk[hh][:, :])
                    v.tensor_copy(f32v(AR4, dst_bc + 1024 + hh * 512, 512), pbk[2 + hh][:, :])
                v.drain()
                v.drain().then_inc(sDV.h, 1); sDV.v += 1

    emit_ln_rows(arow, brow, 4096, True)
    fence()

    def emit_h1(half):
        dv0 = sDV.v
        h_toks = []
        with nc.Block() as blk:
            @blk.sync
            def _(q):
                for ci in range(16):
                    waits = [(sDV.h, dv0 + ci - 1)] if ci >= 2 else []
                    h_toks.append(PL.load(q, f32v(AR4, (ci % 2) * 512, 512),
                                          xT[ci * 128:(ci + 1) * 128, half * 512:(half + 1) * 512],
                                          waits=waits))

            @blk.vector
            def _(v):
                for ci in range(16):
                    v.wait_ge(*h_toks[ci])
                    xc = f32v(AR4, (ci % 2) * 512, 512)
                    t1 = f32v(AR4, 1024, 512)
                    v.tensor_tensor(t1, xc, f32v(AR4, 4096 + half * 512, 512), op=ALU.mult)
                    v.drain()
                    v.tensor_tensor(t1, t1, f32v(AR4, 5120 + half * 512, 512), op=ALU.add)
                    v.drain()
                    v.tensor_scalar(t1, t1, mvec[:, 16 + ci:17 + ci], mvec[:, ci:ci + 1],
                                    op0=ALU.mult, op1=ALU.add)
                    v.drain()
                    v.tensor_copy(bfv(AR2, 0, 8192)[:, ci * 512:(ci + 1) * 512], t1)
                    v.drain()
                    v.scalar_tensor_tensor(bfv(AR2, 4096, 8192)[:, ci * 512:(ci + 1) * 512],
                                           t1, 1.0, bfv(AR2, 0, 8192)[:, ci * 512:(ci + 1) * 512],
                                           op0=ALU.bypass, op1=ALU.subtract)
                    v.drain()
                    v.drain().then_inc(sDV.h, 1); sDV.v += 1

    def emit_proj(half, wsrc, dst_kind):
        h1H = bfv(AR2, 0, 8192)
        h1L = bfv(AR2, 4096, 8192)
        last_spill = [None]
        for fc in range(16):
            tok_box = []
            with nc.Block() as blk:
                @blk.sync
                def _(q):
                    tok_box.append(PL.load(
                        q, f32v(AR2, 8192, 2048).rearrange("p (ci f) -> p ci f", f=128),
                        wsrc[:, fc * 128:(fc + 1) * 128].rearrange("(ci p) f -> p ci f", p=128),
                        waits=[(sDV.h, sDV.v)]))

                @blk.vector
                def _(v):
                    v.wait_ge(*tok_box[0])
                    v.tensor_copy(bfv(AR2, 10240, 2048), f32v(AR2, 8192, 2048))
                    v.drain()
                    v.scalar_tensor_tensor(bfv(AR2, 11264, 2048), f32v(AR2, 8192, 2048),
                                           1.0, bfv(AR2, 10240, 2048),
                                           op0=ALU.bypass, op1=ALU.subtract)
                    v.drain()
                    v.drain().then_inc(sDV.h, 1); sDV.v += 1

                @blk.tensor
                def _(t):
                    t.wait_ge(sDV.h, sDV.v)
                    bank = pbk[fc % 2]
                    first = True
                    for sidx, (wof, mv) in enumerate([(10240, h1H), (10240, h1L), (11264, h1H)]):
                        for ci in range(16):
                            st = bfv(AR2, wof, 2048)[:, ci * 128:(ci + 1) * 128]
                            t.matmul(bank[:, :], st, mv[:, ci * 512:(ci + 1) * 512],
                                     start=first, stop=(sidx == 2 and ci == 15))
                            first = False
                    t.drain().then_inc(sPE.h, 1); sPE.v += 1

            pe_acc = sPE.v
            with nc.Block() as blk:
                @blk.vector
                def _(v):
                    v.wait_ge(sPE.h, pe_acc)
                    if last_spill[0] is not None:
                        v.wait_ge(*last_spill[0])
                    bank = pbk[fc % 2]
                    if dst_kind == 'v':
                        v.tensor_copy(f32v(AR2, 12800, 512), bank[:, :])
                        v.drain()
                    else:
                        v.tensor_copy(bfv(AR2, 12288, 512), bank[:, :])
                        v.drain()
                        v.scalar_tensor_tensor(bfv(AR2, 12544, 512), bank[:, :], 1.0,
                                               bfv(AR2, 12288, 512),
                                               op0=ALU.bypass, op1=ALU.subtract)
                        v.drain()
                    v.drain().then_inc(sDV.h, 1); sDV.v += 1

                @blk.tensor
                def _(t):
                    t.wait_ge(sDV.h, sDV.v)
                    if dst_kind == 'v':
                        for tb in range(4):
                            t.transpose(pbk[2 + tb % 2][:, (tb // 2) * 128:(tb // 2) * 128 + 128],
                                        f32v(AR2, 12800 + tb * 128, 128), ident[:])
                    else:
                        for part, wof in [(0, 12288), (1, 12544)]:
                            pv = pbk[2 + part].ap().bitcast(BF16)
                            for tb in range(4):
                                t.transpose(pv[:, tb * 128:(tb + 1) * 128],
                                            bfv(AR2, wof, 512)[:, tb * 128:(tb + 1) * 128],
                                            identb[:])
                    t.drain().then_inc(sPE.h, 1); sPE.v += 1

                @blk.vector
                def _(v):
                    v.wait_ge(sPE.h, sPE.v)
                    if dst_kind == 'v':
                        for tb in range(4):
                            v.tensor_copy(f32v(AR2, 13312, 512)[:, tb * 128:(tb + 1) * 128],
                                          pbk[2 + tb % 2][:, (tb // 2) * 128:(tb // 2) * 128 + 128])
                        v.drain()
                    else:
                        qk = 0 if dst_kind == 'qh' else 8192
                        for part, arof in [(0, qk), (1, qk + 4096)]:
                            dst = bfv(AR1, arof, 8192)
                            pv = pbk[2 + part].ap().bitcast(BF16)[:, 0:512]
                            v.tensor_copy(
                                dst.rearrange("p (g c) -> p g c", c=2048)[:, :, fc * 128:(fc + 1) * 128],
                                pv.rearrange("p (g c) -> p g c", c=128))
                        v.drain()
                    v.drain().then_inc(sDV.h, 1); sDV.v += 1

                if dst_kind == 'v':
                    @blk.sync
                    def _(q):
                        last_spill[0] = PL.load(
                            q,
                            vscr[:, 0:8192].rearrange("p (g c) -> p g c", c=2048)[:, :, fc * 128:(fc + 1) * 128],
                            f32v(AR2, 13312, 512).rearrange("p (g c) -> p g c", c=128),
                            waits=[(sDV.h, sDV.v)])

    def emit_group(half, g):
        fence()
        QH = bfv(AR1, 0, 8192)[:, g * 2048:(g + 1) * 2048]
        QL = bfv(AR1, 4096, 8192)[:, g * 2048:(g + 1) * 2048]
        KH = bfv(AR1, 8192, 8192)[:, g * 2048:(g + 1) * 2048]
        KL = bfv(AR1, 12288, 8192)[:, g * 2048:(g + 1) * 2048]
        qmh, qml = bfv(AR2, 0, 2048), bfv(AR2, 1024, 2048)
        kmh, kml = bfv(AR2, 2048, 2048), bfv(AR2, 3072, 2048)
        vG = f32v(AR4, 0, 2048)
        vm, vm2 = f32v(AR2, 4096, 2048), f32v(AR2, 6144, 2048)
        attsb, atts2 = f32v(AR2, 8192, 1024), f32v(AR2, 9216, 1024)
        ddup, dT, dT2 = f32v(AR2, 10240, 1024), f32v(AR2, 11264, 1024), f32v(AR2, 12288, 1024)
        AM = f32v(AR2, 13312, 2048)
        AMsq = f32v(AR2, 8192, 2048)
        rmx, rsm = f32v(AR2, 15360, 32), f32v(AR2, 15392, 32)
        s2a, s2b = f32v(AR2, 15424, 32), f32v(AR2, 15456, 32)

        tok_box = []
        with nc.Block() as blk:
            @blk.sync
            def _(q):
                tok_box.append(PL.load(q, vG, vscr[:, g * 2048:(g + 1) * 2048]))

            @blk.vector
            def _(v):
                v.wait_ge(*tok_box[0])
                v.transpose(qmh, QH)
                v.transpose(qml, QL)
                v.transpose(kmh, KH)
                v.transpose(kml, KL)
                v.transpose(vm, vG)
                v.drain()
                v.stream_shuffle(vm2, vm, [(p + 16) % 32 for p in range(32)])
                v.drain()
                v.drain().then_inc(sDV.h, 1); sDV.v += 1

        dv_min = sDV.v
        with nc.Block() as blk:
            @blk.tensor
            def _(t):
                t.wait_ge(sDV.h, dv_min)
                for tt in range(128):
                    pq = tt // 32
                    t32 = tt % 32
                    bank = pbk[4] if t32 < 16 else pbk[5]
                    col = (t32 % 16) * 32
                    idx = 0
                    for qm, km in [(qmh, kmh), (qmh, kml), (qml, kmh)]:
                        for dh in range(2):
                            st = qm[32 * pq:32 * pq + 32, :].rearrange(
                                "p (i c) -> p i c", c=64)[:, :, 32 * dh + t32]
                            mv = km[32 * pq:32 * pq + 32, :].rearrange(
                                "p (i c) -> p i c", c=64)[:, :, 32 * dh + t32]
                            t.matmul(bank[32 * pq:32 * pq + 32, col:col + 32], st, mv,
                                     start=(idx == 0), stop=(idx == 5),
                                     tile_position=(32 * pq, 32 * pq))
                            idx += 1
                t.drain().then_inc(sPE.h, 1); sPE.v += 1

        pe_gram = sPE.v
        with nc.Block() as blk:
            @blk.vector
            def _(v):
                v.wait_ge(sPE.h, pe_gram)
                v.tensor_scalar(attsb[:, 0:512], pbk[4][:, :], SCALE, None, op0=ALU.mult)
                v.tensor_scalar(attsb[:, 512:1024], pbk[5][:, :], SCALE, None, op0=ALU.mult)
                v.drain()
                v.tensor_reduce(rmx, attsb.rearrange("p (s j) -> p s j", j=32),
                                axis=AX.X, op=ALU.max)
                v.drain()
                v.tensor_tensor(atts2.rearrange("p (s j) -> p s j", j=32),
                                attsb.rearrange("p (s j) -> p s j", j=32),
                                rmx.unsqueeze(2).broadcast_to((128, 32, 32)),
                                op=ALU.subtract)
                v.drain()
                v.drain().then_inc(sDV.h, 1); sDV.v += 1

            @blk.scalar
            def _(s):
                s.wait_ge(sDV.h, sDV.v)
                s.activation(attsb[:, :], atts2[:, :], AF.Exp)
                s.drain()
                s.drain().then_inc(sAC.h, 1); sAC.v += 1

            @blk.vector
            def _(v):
                v.wait_ge(sAC.h, sAC.v)
                v.tensor_reduce(rsm, attsb.rearrange("p (s j) -> p s j", j=32),
                                axis=AX.X, op=ALU.add)
                v.drain()
                v.reciprocal(rsm, rsm)
                v.drain()
                v.tensor_tensor(atts2.rearrange("p (s j) -> p s j", j=32),
                                attsb.rearrange("p (s j) -> p s j", j=32),
                                rsm.unsqueeze(2).broadcast_to((128, 32, 32)), op=ALU.mult)
                v.drain()
                v.stream_shuffle(attsb, atts2, [(p + 16) % 32 for p in range(32)])
                v.drain()
                for pq in range(4):
                    for dcp in range(2):
                        v.scalar_tensor_tensor(
                            ddup[32 * pq:32 * pq + 16, :].rearrange(
                                "p (s j) -> p s j", j=32)[:, :, dcp * 16:(dcp + 1) * 16],
                            attsb[32 * pq:32 * pq + 16, :].rearrange(
                                "p (s j) -> p s j", j=32)[:, :, 16:32],
                            ppt[32 * pq:32 * pq + 16, 0:1],
                            atts2[32 * pq:32 * pq + 16, :].rearrange(
                                "p (s j) -> p s j", j=32)[:, :, 0:16],
                            op0=ALU.mult, op1=ALU.add)
                        v.drain()
                v.transpose(dT, ddup)
                v.drain()
                v.tensor_copy(dT2.rearrange("p (s j) -> p s j", j=32)[:, :, 16:32],
                              dT.rearrange("p (s j) -> p s j", j=32)[:, :, 0:16])
                v.drain()
                v.drain().then_inc(sDV.h, 1); sDV.v += 1

        dv_diff = sDV.v
        pe_dv0 = sPE.v
        dv_ev0 = sDV.v
        with nc.Block() as blk:
            @blk.tensor
            def _(t):
                t.wait_ge(sDV.h, dv_diff)
                for batch in range(4):
                    if batch >= 2:
                        t.wait_ge(sDV.h, dv_ev0 + batch - 1)
                    for w in range(8):
                        s = batch * 8 + w
                        for pq in range(4):
                            be = pbk[6] if batch % 2 == 0 else pbk[7]
                            st_e = dT[32 * pq:32 * pq + 16, s * 32:s * 32 + 32]
                            st_o = dT2[32 * pq:32 * pq + 16, s * 32:s * 32 + 32]
                            mv_e = vm[32 * pq:32 * pq + 16, :].rearrange(
                                "p (u c) -> p u c", c=32)[:, :, s]
                            mv_o = vm2[32 * pq:32 * pq + 16, :].rearrange(
                                "p (u c) -> p u c", c=32)[:, :, s]
                            t.matmul(be[32 * pq:32 * pq + 32, w * 64:w * 64 + 64], st_e, mv_e,
                                     start=True, stop=False, tile_position=(32 * pq, 32 * pq))
                            t.matmul(be[32 * pq:32 * pq + 32, w * 64:w * 64 + 64], st_o, mv_o,
                                     start=False, stop=True, tile_position=(32 * pq, 32 * pq))
                    t.drain().then_inc(sPE.h, 1); sPE.v += 1

            @blk.vector
            def _(v):
                for batch in range(4):
                    v.wait_ge(sPE.h, pe_dv0 + batch + 1)
                    be = pbk[6] if batch % 2 == 0 else pbk[7]
                    v.tensor_copy(
                        AM.rearrange("p (u c) -> p u c", c=32)[:, :, batch * 8:batch * 8 + 8],
                        be[:, :].rearrange("p (w u) -> p u w", u=64))
                    v.drain().then_inc(sDV.h, 1); sDV.v += 1

        fence()
        with nc.Block() as blk:
            @blk.scalar
            def _(s):
                s.activation(AMsq, AM, AF.Square)
                s.drain()
                s.drain().then_inc(sAC.h, 1); sAC.v += 1

            @blk.vector
            def _(v):
                v.wait_ge(sAC.h, sAC.v)
                v.tensor_reduce(s2a, AMsq.rearrange("p (u c) -> p c u", c=32),
                                axis=AX.X, op=ALU.add)
                v.drain()
                v.stream_shuffle(s2b, s2a, [(p + 16) % 32 for p in range(32)])
                v.drain()
                v.tensor_tensor(s2a, s2a, s2b, op=ALU.add)
                v.drain()
                v.tensor_scalar(s2a, s2a, 1.0 / 128.0, EPS, op0=ALU.mult, op1=ALU.add)
                v.drain()
                v.reciprocal(s2a, s2a)
                v.drain()
                v.drain().then_inc(sDV.h, 1); sDV.v += 1

            @blk.scalar
            def _(s):
                s.wait_ge(sDV.h, sDV.v)
                s.activation(s2a, s2a, AF.Sqrt, scale=(1.0 - LAM_INIT) ** 2)
                s.drain()
                s.drain().then_inc(sAC.h, 1); sAC.v += 1

            @blk.vector
            def _(v):
                v.wait_ge(sAC.h, sAC.v)
                v.tensor_tensor(AM.rearrange("p (u c) -> p u c", c=32),
                                AM.rearrange("p (u c) -> p u c", c=32),
                                s2a.unsqueeze(1).broadcast_to((128, 64, 32)), op=ALU.mult)
                v.drain()
                v.tensor_copy(bfv(AR2, 8192, 2048), AM)
                v.drain()
                v.scalar_tensor_tensor(bfv(AR2, 9216, 2048), AM, 1.0, bfv(AR2, 8192, 2048),
                                       op0=ALU.bypass, op1=ALU.subtract)
                v.drain()
                v.transpose(bfv(AR2, 8192, 2048), bfv(AR2, 8192, 2048))
                v.drain()
                v.transpose(bfv(AR2, 9216, 2048), bfv(AR2, 9216, 2048))
                v.drain()
                v.drain().then_inc(sDV.h, 1); sDV.v += 1

            dv_t32 = sDV.v
            @blk.tensor
            def _(t):
                t.wait_ge(sDV.h, dv_t32)
                i = 0
                for part, c0 in [(0, 8192), (1, 9216)]:
                    for fq in range(4):
                        if i >= 2:
                            t.wait_ge(sDV.h, dv_t32 + i - 1)
                        bank = pbk[part * 2 + fq % 2].ap().bitcast(BF16)
                        for sub in range(4):
                            fc = fq * 4 + sub
                            t.transpose(bank[:, sub * 128:(sub + 1) * 128],
                                        bfv(AR2, c0, 2048)[:, fc * 128:(fc + 1) * 128],
                                        identb[:])
                        t.drain().then_inc(sPE.h, 1); sPE.v += 1
                        i += 1

            @blk.vector
            def _(v):
                cnt = sPE.v - 8
                for part, arof in [(0, 0), (1, 4096)]:
                    for fq in range(4):
                        cnt += 1
                        v.wait_ge(sPE.h, cnt)
                        bank = pbk[part * 2 + fq % 2].ap().bitcast(BF16)[:, 0:512]
                        dst = bfv(AR3, arof, 8192)
                        v.tensor_copy(
                            dst.rearrange("p (fc t) -> p fc t", t=512)[:, fq * 4:fq * 4 + 4,
                                                                       g * 128:(g + 1) * 128],
                            bank.rearrange("p (fc t) -> p fc t", t=128))
                        v.drain().then_inc(sDV.h, 1); sDV.v += 1

    def emit_wo(half):
        AFMH = bfv(AR3, 0, 8192)
        AFML = bfv(AR3, 4096, 8192)
        last_spill = [None]
        for fc in range(16):
            tok_box = []
            with nc.Block() as blk:
                @blk.sync
                def _(q):
                    tok_box.append(PL.load(
                        q, f32v(AR2, 8192, 2048).rearrange("p (ci f) -> p ci f", f=128),
                        woT[:, fc * 128:(fc + 1) * 128].rearrange("(ci p) f -> p ci f", p=128),
                        waits=[(sDV.h, sDV.v)]))
                    tok_box.append(PL.load(
                        q, f32v(AR4, (fc % 2) * 512, 512),
                        xT[fc * 128:(fc + 1) * 128, half * 512:(half + 1) * 512]))

                @blk.vector
                def _(v):
                    v.wait_ge(*tok_box[0])
                    v.tensor_copy(bfv(AR2, 10240, 2048), f32v(AR2, 8192, 2048))
                    v.drain()
                    v.scalar_tensor_tensor(bfv(AR2, 11264, 2048), f32v(AR2, 8192, 2048),
                                           1.0, bfv(AR2, 10240, 2048),
                                           op0=ALU.bypass, op1=ALU.subtract)
                    v.drain()
                    v.drain().then_inc(sDV.h, 1); sDV.v += 1

                @blk.tensor
                def _(t):
                    t.wait_ge(sDV.h, sDV.v)
                    bank = pbk[fc % 2]
                    first = True
                    for sidx, (wof, mv) in enumerate([(10240, AFMH), (10240, AFML), (11264, AFMH)]):
                        for ci in range(16):
                            st = bfv(AR2, wof, 2048)[:, ci * 128:(ci + 1) * 128]
                            t.matmul(bank[:, :], st, mv[:, ci * 512:(ci + 1) * 512],
                                     start=first, stop=(sidx == 2 and ci == 15))
                            first = False
                    t.drain().then_inc(sPE.h, 1); sPE.v += 1

            pe_acc = sPE.v
            with nc.Block() as blk:
                @blk.scalar
                def _(s):
                    s.wait_ge(sPE.h, pe_acc)
                    s.wait_ge(*tok_box[1])
                    s.activation(f32v(AR4, 1024, 512), pbk[fc % 2][:, :], AF.Identity,
                                 bias=bofm[:, fc:fc + 1], scale=1.0)
                    s.drain()
                    s.drain().then_inc(sAC.h, 1); sAC.v += 1

                @blk.vector
                def _(v):
                    v.wait_ge(sAC.h, sAC.v)
                    if last_spill[0] is not None:
                        v.wait_ge(*last_spill[0])
                    v.scalar_tensor_tensor(f32v(AR4, 2048, 512), f32v(AR4, 1024, 512),
                                           mvec[:, 32 + fc:33 + fc], f32v(AR4, (fc % 2) * 512, 512),
                                           op0=ALU.mult, op1=ALU.add)
                    v.drain()
                    v.drain().then_inc(sDV.h, 1); sDV.v += 1

                @blk.scalar
                def _(s):
                    s.wait_ge(sDV.h, sDV.v)
                    s.activation(f32v(AR4, 3072, 512), f32v(AR4, 2048, 512), AF.Square)
                    s.drain()
                    s.drain().then_inc(sAC.h, 1); sAC.v += 1

                @blk.tensor
                def _(t):
                    t.wait_ge(sAC.h, sAC.v)
                    t.matmul(pbk[4][0:1, :], onespc[:], f32v(AR4, 2048, 512),
                             start=(fc == 0), stop=(fc == 15))
                    t.matmul(pbk[6][0:1, :], onespc[:], f32v(AR4, 3072, 512),
                             start=(fc == 0), stop=(fc == 15))
                    t.drain().then_inc(sPE.h, 1); sPE.v += 1

                @blk.sync
                def _(q):
                    last_spill[0] = PL.load(
                        q, x2scr[fc * 128:(fc + 1) * 128, half * 512:(half + 1) * 512],
                        f32v(AR4, 2048, 512), waits=[(sPE.h, sPE.v)])
        fence()
        with nc.Block() as blk:
            @blk.vector
            def _(v):
                v.tensor_scalar(srow[:, half * 512:(half + 1) * 512], pbk[4][0:1, :],
                                1.0 / DIM, None, op0=ALU.mult)
                v.tensor_scalar(qrow[:, half * 512:(half + 1) * 512], pbk[6][0:1, :],
                                1.0 / DIM, None, op0=ALU.mult)
                v.drain()
                v.drain().then_inc(sDV.h, 1); sDV.v += 1

    for half in range(2):
        emit_h1(half)
        fence()
        emit_proj(half, wqT, 'qh')
        emit_proj(half, wkT, 'k')
        emit_proj(half, wvT, 'v')
        fence()
        with nc.Block() as blk:
            @blk.vector
            def _(v):
                v.memset(f32v(AR2, 10240, 1024), 0.0)
                v.memset(f32v(AR2, 12288, 1024), 0.0)
                v.drain()
                v.drain().then_inc(sDV.h, 1); sDV.v += 1
        for g in range(4):
            emit_group(half, g)
        fence()
        emit_wo(half)
        fence()

    emit_ln_rows(a2row, b2row, 4096, False)
    fence()

    def utile(k):
        return f32v(AR1, k * 512, 512) if k < 32 else f32v(AR2, (k - 32) * 512, 512)

    def zqv(k):
        if k == 0:
            return zq0t.ap().bitcast(BF16)[:, 0:512]
        return AR1.ap().bitcast(BF16)[:, (k - 1) * 512:k * 512]

    def emit_pmax_bcast(srcdst):
        with nc.Block() as blk:
            @blk.tensor
            def _(t):
                t.wait_ge(sDV.h, sDV.v)
                for tb in range(4):
                    t.transpose(pbk[3][:, tb * 128:(tb + 1) * 128],
                                srcdst[:, tb * 128:(tb + 1) * 128], ident[:])
                t.drain().then_inc(sPE.h, 1); sPE.v += 1

            @blk.vector
            def _(v):
                v.wait_ge(sPE.h, sPE.v)
                v.tensor_copy(qt1[:], pbk[3][:, :])
                v.drain()
                v.tensor_reduce(qt2[:, 0:4], qt1.ap().rearrange("p (tb c) -> p tb c", c=128),
                                axis=AX.X, op=ALU.max)
                v.drain().then_inc(sDV.h, 1); sDV.v += 1

            @blk.tensor
            def _(t):
                t.wait_ge(sDV.h, sDV.v)
                for tb in range(4):
                    t.transpose(pbk[4][0:1, tb * 128:(tb + 1) * 128],
                                qt2[:, tb:tb + 1], ident[:])
                t.drain().then_inc(sPE.h, 1); sPE.v += 1

            @blk.vector
            def _(v):
                v.wait_ge(sPE.h, sPE.v)
                v.tensor_copy(qt1[0:1, 0:512], pbk[4][0:1, :])
                v.drain().then_inc(sDV.h, 1); sDV.v += 1

            @blk.tensor
            def _(t):
                t.wait_ge(sDV.h, sDV.v)
                t.matmul(pbk[3][:, :], onescr[:], qt1[0:1, 0:512], start=True, stop=True)
                t.drain().then_inc(sPE.h, 1); sPE.v += 1

            @blk.vector
            def _(v):
                v.wait_ge(sPE.h, sPE.v)
                v.tensor_copy(srcdst, pbk[3][:, :])
                v.drain().then_inc(sDV.h, 1); sDV.v += 1

    def emit_h2q(half):
        dv0 = sDV.v
        h_toks = []
        with nc.Block() as blk:
            @blk.sync
            def _(q):
                for ci in range(16):
                    waits = [(sDV.h, dv0 + ci - 1)] if ci >= 2 else []
                    h_toks.append(PL.load(
                        q, f32v(AR4, (ci % 2) * 512, 512),
                        x2scr[ci * 128:(ci + 1) * 128, half * 512:(half + 1) * 512],
                        waits=waits))

            @blk.vector
            def _(v):
                for ci in range(16):
                    v.wait_ge(*h_toks[ci])
                    xc = f32v(AR4, (ci % 2) * 512, 512)
                    t1 = f32v(AR1, 8192 + (ci % 2) * 512, 512)
                    v.tensor_tensor(t1, xc, f32v(AR4, 4096 + half * 512, 512), op=ALU.mult)
                    v.drain()
                    v.tensor_tensor(t1, t1, f32v(AR4, 5120 + half * 512, 512), op=ALU.add)
                    v.drain()
                    v.tensor_scalar(t1, t1, mvec[:, 64 + ci:65 + ci], mvec[:, 48 + ci:49 + ci],
                                    op0=ALU.mult, op1=ALU.add)
                    v.drain()
                    v.tensor_copy(f32v(AR2, ci * 512, 512), t1)
                    v.drain()
                    if ci == 0:
                        v.tensor_copy(f32v(AR4, 1536, 512), t1)
                        v.drain()
                        v.tensor_copy(f32v(AR1, 9216, 512), t1)
                    else:
                        v.tensor_tensor(f32v(AR4, 1536, 512), f32v(AR4, 1536, 512), t1, op=ALU.max)
                        v.drain()
                        v.tensor_tensor(f32v(AR1, 9216, 512), f32v(AR1, 9216, 512), t1, op=ALU.min)
                    v.drain()
                    v.drain().then_inc(sDV.h, 1); sDV.v += 1

        fence()
        with nc.Block() as blk:
            @blk.vector
            def _(v):
                v.tensor_scalar(qt1[:], f32v(AR1, 9216, 512), -1.0, None, op0=ALU.mult)
                v.drain()
                v.tensor_tensor(f32v(AR4, 1536, 512), f32v(AR4, 1536, 512), qt1[:], op=ALU.max)
                v.drain().then_inc(sDV.h, 1); sDV.v += 1
        emit_pmax_bcast(f32v(AR4, 1536, 512))
        with nc.Block() as blk:
            @blk.vector
            def _(v):
                v.wait_ge(sDV.h, sDV.v)
                v.tensor_scalar(f32v(AR4, 1536, 512), f32v(AR4, 1536, 512), EPS, None, op0=ALU.max)
                v.drain()
                v.reciprocal(f32v(AR4, 2048, 512), f32v(AR4, 1536, 512))
                v.drain()
                v.tensor_scalar(f32v(AR4, 2048, 512), f32v(AR4, 2048, 512), QB, None, op0=ALU.mult)
                v.drain()
                v.tensor_scalar(f32v(AR4, 2560, 512), f32v(AR4, 1536, 512), ppt[:, 1:2],
                                1.0 / QB, op0=ALU.mult, op1=ALU.mult)
                v.tensor_scalar(f32v(AR4, 3072, 512), f32v(AR4, 1536, 512), ppt[:, 2:3],
                                1.0 / QB, op0=ALU.mult, op1=ALU.mult)
                v.drain()
                for ci in range(16):
                    hc = f32v(AR2, ci * 512, 512)
                    v.tensor_tensor(qt1[:], hc, f32v(AR4, 2048, 512), op=ALU.mult)
                    v.drain()
                    v.tensor_scalar(qt1[:], qt1[:], 127.0, 384.5, op0=ALU.min, op1=ALU.add)
                    v.drain()
                    v.tensor_copy(qt3[:], qt1[:])
                    v.drain()
                    v.tensor_copy(qt2[:], qt3[:])
                    v.drain()
                    v.tensor_tensor(qt1[:], qt1[:], qt2[:], op=ALU.subtract)
                    v.drain()
                    v.scalar_tensor_tensor(qt2[:], qt1[:], 1.0, qt2[:], op0=ALU.is_ge, op1=ALU.add)
                    v.drain()
                    v.scalar_tensor_tensor(qt2[:], qt1[:], 0.0, qt2[:], op0=ALU.is_ge, op1=ALU.add)
                    v.drain()
                    v.tensor_scalar(bfv(AR3, 0, 8192)[:, ci * 512:(ci + 1) * 512],
                                    qt2[:], -385.0, None, op0=ALU.add)
                    v.drain()
                v.drain().then_inc(sDV.h, 1); sDV.v += 1

    def emit_tern(w32, dst_bf, kindcol, waits):
        ttmp = bfv(AR4, 7168, 2048)
        with nc.Block() as blk:
            @blk.vector
            def _(v):
                for h, vv in waits:
                    v.wait_ge(h, vv)
                v.tensor_scalar(ttmp, w32, ppt[:, 4 + kindcol:5 + kindcol], None, op0=ALU.is_gt)
                v.drain()
                v.tensor_scalar(dst_bf, w32, ppt[:, 7 + kindcol:8 + kindcol], -1.0,
                                op0=ALU.is_gt, op1=ALU.add)
                v.drain()
                v.tensor_tensor(dst_bf, dst_bf, ttmp, op=ALU.add)
                v.drain()
                v.drain().then_inc(sDV.h, 1); sDV.v += 1

    def emit_ffn(half):
        emit_h2q(half)
        fence()
        h2q = bfv(AR3, 0, 8192)
        spill_toks = {6144: None, 7168: None}

        def stream_tern(q_is_new_block, src_f32, tern_scr, scr_off, tern_of, kindcol):
            pass

        for oc in range(64):
            tern_of = 6144 if oc % 2 == 0 else 7168
            for which, wsrcT, scr, kindcol, banki in [(0, wgT, tG, 0, 0), (1, wuT, tU, 1, 1)]:
                tok_box = []
                with nc.Block() as blk:
                    @blk.sync
                    def _(q):
                        waits = [(sPE.h, sPE.v), (sDV.h, sDV.v)]
                        if half == 0:
                            tok_box.append(PL.load(
                                q, f32v(AR3, 4096, 2048).rearrange("p (ci f) -> p ci f", f=128),
                                wsrcT[:, oc * 128:(oc + 1) * 128].rearrange("(ci p) f -> p ci f", p=128),
                                waits=waits))
                        else:
                            tok_box.append(PL.load(
                                q, bfv(AR3, tern_of, 2048),
                                scr[:, oc * 2048:(oc + 1) * 2048], waits=waits))

                if half == 0:
                    tw = [tok_box[0]]
                    if spill_toks[tern_of] is not None:
                        tw.append(spill_toks[tern_of])
                    emit_tern(f32v(AR3, 4096, 2048), bfv(AR3, tern_of, 2048), kindcol, tw)
                    with nc.Block() as blk:
                        @blk.sync
                        def _(q):
                            spill_toks[tern_of] = PL.load(
                                q, scr[:, oc * 2048:(oc + 1) * 2048],
                                bfv(AR3, tern_of, 2048), waits=[(sDV.h, sDV.v)])

                with nc.Block() as blk:
                    @blk.tensor
                    def _(t):
                        if half == 0:
                            t.wait_ge(sDV.h, sDV.v)
                        else:
                            t.wait_ge(*tok_box[0])
                        for ci in range(16):
                            t.matmul(pbk[banki][:, :],
                                     bfv(AR3, tern_of, 2048)[:, ci * 128:(ci + 1) * 128],
                                     h2q[:, ci * 512:(ci + 1) * 512],
                                     start=(ci == 0), stop=(ci == 15))
                        t.drain().then_inc(sPE.h, 1); sPE.v += 1

            pe_gu = sPE.v
            with nc.Block() as blk:
                @blk.vector
                def _(v):
                    v.wait_ge(sPE.h, pe_gu)
                    v.tensor_tensor(qt1[:], pbk[0][:, :], f32v(AR4, 2560, 512), op=ALU.mult)
                    v.drain()
                    v.drain().then_inc(sDV.h, 1); sDV.v += 1

                @blk.scalar
                def _(s):
                    s.wait_ge(sDV.h, sDV.v)
                    s.activation(qt1[:], qt1[:], AF.Silu)
                    s.drain()
                    s.drain().then_inc(sAC.h, 1); sAC.v += 1

                @blk.vector
                def _(v):
                    v.wait_ge(sAC.h, sAC.v)
                    v.tensor_tensor(qt2[:], pbk[1][:, :], f32v(AR4, 3072, 512), op=ALU.mult)
                    v.drain()
                    v.tensor_tensor(utile(oc), qt1[:], qt2[:], op=ALU.mult)
                    v.drain()
                    v.drain().then_inc(sDV.h, 1); sDV.v += 1

        fence()
        pe0_ = sPE.v
        dv0_ = sDV.v
        with nc.Block() as blk:
            @blk.tensor
            def _(t):
                for k in range(64):
                    if k >= 2:
                        t.wait_ge(sDV.h, dv0_ + k - 1)
                    t.matmul(pbk[2 + k % 2][:, :], hadt[:], utile(k), start=True, stop=True)
                    t.drain().then_inc(sPE.h, 1); sPE.v += 1

            @blk.vector
            def _(v):
                for k in range(64):
                    v.wait_ge(sPE.h, pe0_ + k + 1)
                    v.tensor_copy(utile(k), pbk[2 + k % 2][:, :])
                    v.drain()
                    v.drain().then_inc(sDV.h, 1); sDV.v += 1

        fence()
        with nc.Block() as blk:
            @blk.vector
            def _(v):
                for s in range(6):
                    d = 1 << s
                    for base in range(0, 64, 2 * d):
                        for off in range(d):
                            a, b = base + off, base + off + d
                            v.tensor_tensor(utile(a), utile(a), utile(b), op=ALU.add)
                            v.drain()
                            v.scalar_tensor_tensor(utile(b), utile(b), -2.0, utile(a),
                                                   op0=ALU.mult, op1=ALU.add)
                            v.drain()
                v.drain().then_inc(sDV.h, 1); sDV.v += 1

        fence()
        with nc.Block() as blk:
            @blk.vector
            def _(v):
                v.tensor_copy(f32v(AR4, 1536, 512), utile(0))
                v.drain()
                v.tensor_copy(f32v(AR4, 0, 512), utile(0))
                v.drain()
                for k in range(1, 64):
                    v.tensor_tensor(f32v(AR4, 1536, 512), f32v(AR4, 1536, 512), utile(k), op=ALU.max)
                    v.drain()
                    v.tensor_tensor(f32v(AR4, 0, 512), f32v(AR4, 0, 512), utile(k), op=ALU.min)
                    v.drain()
                v.tensor_scalar(qt1[:], f32v(AR4, 0, 512), -1.0, None, op0=ALU.mult)
                v.drain()
                v.tensor_tensor(f32v(AR4, 1536, 512), f32v(AR4, 1536, 512), qt1[:], op=ALU.max)
                v.drain().then_inc(sDV.h, 1); sDV.v += 1

        fence()
        emit_pmax_bcast(f32v(AR4, 1536, 512))
        with nc.Block() as blk:
            @blk.vector
            def _(v):
                v.wait_ge(sDV.h, sDV.v)
                v.tensor_scalar(f32v(AR4, 1536, 512), f32v(AR4, 1536, 512), C_FW, EPS,
                                op0=ALU.mult, op1=ALU.max)
                v.drain()
                v.reciprocal(f32v(AR4, 2048, 512), f32v(AR4, 1536, 512))
                v.drain()
                v.tensor_scalar(f32v(AR4, 2048, 512), f32v(AR4, 2048, 512), QB * C_FW, None,
                                op0=ALU.mult)
                v.tensor_scalar(f32v(AR4, 3584, 512), f32v(AR4, 1536, 512), ppt[:, 3:4],
                                1.0 / QB, op0=ALU.mult, op1=ALU.mult)
                v.drain()
                for k in range(64):
                    v.tensor_tensor(qt1[:], utile(k), f32v(AR4, 2048, 512), op=ALU.mult)
                    v.drain()
                    v.tensor_scalar(qt1[:], qt1[:], 127.0, 384.5, op0=ALU.min, op1=ALU.add)
                    v.drain()
                    v.tensor_copy(qt3[:], qt1[:])
                    v.drain()
                    v.tensor_copy(qt2[:], qt3[:])
                    v.drain()
                    v.tensor_tensor(qt1[:], qt1[:], qt2[:], op=ALU.subtract)
                    v.drain()
                    v.scalar_tensor_tensor(qt2[:], qt1[:], 1.0, qt2[:], op0=ALU.is_ge, op1=ALU.add)
                    v.drain()
                    v.scalar_tensor_tensor(qt2[:], qt1[:], 0.0, qt2[:], op0=ALU.is_ge, op1=ALU.add)
                    v.drain()
                    v.tensor_scalar(zqv(k), qt2[:], -385.0, None, op0=ALU.add)
                    v.drain()
                v.drain().then_inc(sDV.h, 1); sDV.v += 1

        fence()
        last_y = [None]
        for fo in range(16):
            tern_of = 6144 if fo % 2 == 0 else 7168
            for sub in range(4):
                tok_box = []
                with nc.Block() as blk:
                    @blk.sync
                    def _(q):
                        waits = [(sPE.h, sPE.v), (sDV.h, sDV.v)]
                        if half == 0:
                            tok_box.append(PL.load(
                                q, f32v(AR3, 4096, 2048).rearrange("p (ci f) -> p ci f", f=128),
                                wdT[sub * 2048:(sub + 1) * 2048,
                                    fo * 128:(fo + 1) * 128].rearrange("(ci p) f -> p ci f", p=128),
                                waits=waits))
                        else:
                            tok_box.append(PL.load(
                                q, bfv(AR3, tern_of, 2048),
                                tD[:, (fo * 4 + sub) * 2048:(fo * 4 + sub + 1) * 2048],
                                waits=waits))

                if half == 0:
                    tw = [tok_box[0]]
                    if spill_toks[tern_of] is not None:
                        tw.append(spill_toks[tern_of])
                    emit_tern(f32v(AR3, 4096, 2048), bfv(AR3, tern_of, 2048), 2, tw)
                    with nc.Block() as blk:
                        @blk.sync
                        def _(q):
                            spill_toks[tern_of] = PL.load(
                                q, tD[:, (fo * 4 + sub) * 2048:(fo * 4 + sub + 1) * 2048],
                                bfv(AR3, tern_of, 2048), waits=[(sDV.h, sDV.v)])

                with nc.Block() as blk:
                    @blk.tensor
                    def _(t):
                        if half == 0:
                            t.wait_ge(sDV.h, sDV.v)
                        else:
                            t.wait_ge(*tok_box[0])
                        for ci in range(16):
                            k = sub * 16 + ci
                            t.matmul(pbk[0][:, :],
                                     bfv(AR3, tern_of, 2048)[:, ci * 128:(ci + 1) * 128],
                                     zqv(k), start=(k == 0), stop=(k == 63))
                        t.drain().then_inc(sPE.h, 1); sPE.v += 1

            tok_box = []
            with nc.Block() as blk:
                @blk.sync
                def _(q):
                    tok_box.append(PL.load(
                        q, f32v(AR4, 0, 512),
                        x2scr[fo * 128:(fo + 1) * 128, half * 512:(half + 1) * 512]))

                @blk.vector
                def _(v):
                    v.wait_ge(sPE.h, sPE.v)
                    v.wait_ge(*tok_box[0])
                    if last_y[0] is not None:
                        v.wait_ge(*last_y[0])
                    v.tensor_tensor(qt1[:], pbk[0][:, :], f32v(AR4, 3584, 512), op=ALU.mult)
                    v.drain()
                    v.scalar_tensor_tensor(qt2[:], qt1[:], mvec[:, 80 + fo:81 + fo], f32v(AR4, 0, 512),
                                           op0=ALU.mult, op1=ALU.add)
                    v.drain()
                    v.drain().then_inc(sDV.h, 1); sDV.v += 1

                @blk.sync
                def _(q):
                    last_y[0] = PO.load(
                        q, y[fo * 128:(fo + 1) * 128, half * 512:(half + 1) * 512],
                        qt2[:], waits=[(sDV.h, sDV.v)])

    emit_ffn(0)
    fence()
    emit_ffn(1)

    with nc.Block() as blk:
        @blk.sync
        def _(q):
            for h, v in PO.toks():
                q.wait_ge(h, v)

    return nc


def kernel(**inputs):
    x = np.asarray(inputs["L0_attn_x"])
    Wq = np.asarray(inputs["L0_attn_Wq"])
    Wk = np.asarray(inputs["L0_attn_Wk"])
    Wv = np.asarray(inputs["L0_attn_Wv"])
    Wo = np.asarray(inputs["L0_attn_Wo"])
    bo = np.asarray(inputs["L0_attn_bo"])
    lq1 = np.asarray(inputs["L0_attn_lq1"])
    lq2 = np.asarray(inputs["L0_attn_lq2"])
    lk1 = np.asarray(inputs["L0_attn_lk1"])
    lk2 = np.asarray(inputs["L0_attn_lk2"])
    wg = np.asarray(inputs["L0_ffn_w_gate"])
    wu = np.asarray(inputs["L0_ffn_w_up"])
    wd = np.asarray(inputs["L0_ffn_w_down"])
    c = np.asarray(inputs["L0_c"])
    wmod = np.asarray(inputs["L0_wmod"])

    pqk = _perm_qk()
    pv = _perm_v()
    pwo = _perm_wo_rows()

    wqT = np.ascontiguousarray(Wq.T[:, pqk])
    wkT = np.ascontiguousarray(Wk.T[:, pqk])
    wvT = np.ascontiguousarray(Wv.T[:, pv])
    woT = np.ascontiguousarray(Wo.T[pwo, :])
    wgT = np.ascontiguousarray(wg.T)
    wuT = np.ascontiguousarray(wu.T)
    wdT = np.ascontiguousarray(wd.T)

    shared = dict(
        wqT=wqT, wkT=wkT, wvT=wvT, woT=woT, wgT=wgT, wuT=wuT, wdT=wdT,
        wmod=np.ascontiguousarray(wmod),
        cvec=np.ascontiguousarray(c.reshape(1, DIM)),
        bo=np.ascontiguousarray(bo.reshape(1, DIM)),
        lq1=np.ascontiguousarray(lq1.reshape(1, D)),
        lq2=np.ascontiguousarray(lq2.reshape(1, D)),
        lk1=np.ascontiguousarray(lk1.reshape(1, D)),
        lk2=np.ascontiguousarray(lk2.reshape(1, D)),
        eye=np.eye(128, dtype=np.float32),
        had=_hadamard(128),
        onesc=np.ones((1, 128), dtype=np.float32),
        onesp=np.ones((128, 1), dtype=np.float32),
    )

    xt = x[0].T
    in_maps = []
    for i in range(NCORE):
        m = dict(shared)
        m["xT"] = np.ascontiguousarray(xt[:, i * TOKC:(i + 1) * TOKC])
        in_maps.append(m)

    nc = build_program()
    res = run_bass_kernel_spmd(nc, in_maps, core_ids=list(range(NCORE)))
    out = np.concatenate([np.asarray(r["y"]).T for r in res.results], axis=0)
    return np.ascontiguousarray(out.reshape(1, 8192, DIM)).astype(np.float32)
